# version 03 (final)
# baseline (speedup 1.0000x reference)
import contextlib
import numpy as np

import concourse.bass as bass
import concourse.mybir as mybir
from concourse.bass_utils import run_bass_kernel_spmd

F32 = mybir.dt.float32
I32 = mybir.dt.int32
AF = mybir.ActivationFunctionType
ALU = mybir.AluOpType
AX = mybir.AxisListType

D = 1024
DC = 8
T = 2048
NTT = 4
S = 4096
G = 8
KT = 32
QBN = 4
HC0 = 24
HC1 = 32
EPS = 1e-5
NEGBIG = -640000.0
SC0 = 0.125
SC1 = 1.0 / 64.0
C127 = float(np.float32(1.0 / 127.0))

ENG_ORDER = ("sync", "tensor", "vector", "scalar", "gpsimd")


class _Op:
    __slots__ = ("eng", "stream", "fn", "waits", "fixed_waits", "inc", "needed", "val", "gidx")

    def __init__(self, eng, stream, fn, waits, inc, gidx):
        self.eng, self.stream, self.fn = eng, stream, fn
        self.waits, self.inc, self.gidx = waits, inc, gidx
        self.fixed_waits = []
        self.needed, self.val = False, 0


class Buf:
    __slots__ = ("w", "r")

    def __init__(self, seed=()):
        self.w = list(seed)
        self.r = []


COMPUTE_STREAMS = ("t", "v", "s", "g")


class Prog:

    def __init__(self):
        self.by_eng = {e: [] for e in ENG_ORDER}
        self.n = 0
        self.scnt = {}

    def emit(self, eng, fn, reads=(), writes=(), waits=(), inc=1, stream=None):
        if stream is None:
            assert eng != "sync", "sync ops need an explicit DMA stream"
            stream = eng[0]
        deps = list(waits)
        for b in reads:
            deps += b.w
        for b in writes:
            deps += b.r + b.w
        op = _Op(eng, stream, fn, deps, inc, self.n)
        self.n += 1
        fixed = {}
        kept = []
        for d in deps:
            assert d.gidx < op.gidx, "wait on later op (deadlock)"
            d.needed = True
            if d.stream in COMPUTE_STREAMS:
                kept.append(d)
            else:
                fixed[d.stream] = self.scnt[d.stream]
        if stream not in COMPUTE_STREAMS:
            op.needed = True
            prev = self.scnt.get(stream, 0)
            if prev and fixed.get(stream, 0) < prev:
                fixed[stream] = prev
            self.scnt[stream] = prev + inc
        op.waits = kept
        op.fixed_waits = [(k, v) for k, v in fixed.items()]
        self.by_eng[eng].append(op)
        for b in reads:
            b.r.append(op)
        for b in writes:
            b.w = [op]
            b.r = []
        return op

    def assign_vals(self):
        cnt = {}
        for e in ENG_ORDER:
            for op in self.by_eng[e]:
                if op.needed:
                    cnt[op.stream] = cnt.get(op.stream, 0) + op.inc
                    op.val = cnt[op.stream]

    def emit_engine(self, ename, eng, sems):
        seen = {}
        for op in self.by_eng[ename]:
            tgt = {}
            for d in op.waits:
                if d.val and d.val > tgt.get(d.stream, 0):
                    tgt[d.stream] = d.val
            for k, v in op.fixed_waits:
                if v > tgt.get(k, 0):
                    tgt[k] = v
            for k in sorted(tgt):
                if tgt[k] > seen.get(k, 0):
                    eng.wait_ge(sems[k], tgt[k])
                    seen[k] = tgt[k]
            inst = op.fn(eng)
            if op.needed:
                inst.then_inc(sems[op.stream], op.inc)


def build_program(n_taps=0, n_sub=6, attn_parts=9, dbg_attn=-1):
    nc = bass.Bass("TRN2", target_bir_lowering=False, debug=False)

    def din(name, shape, dtype=F32):
        return nc.dram_tensor(name, list(shape), dtype, kind="ExternalInput").ap()

    x_in = din("x_fm", [D, T])
    w = {}
    for nm_, nk_, name, dty in [
        (8, 8, "l0_wq", F32), (8, 8, "l0_wk", F32), (8, 8, "l0_wv", F32), (8, 8, "l0_wo", F32),
        (HC0, 8, "l0_wg", I32), (HC0, 8, "l0_wu", I32), (8, HC0, "l0_wd", I32),
        (8, 8, "l1_wq", F32), (8, 8, "l1_wk", F32), (8, 8, "l1_wv", F32), (8, 8, "l1_wo", F32),
        (HC1, 8, "l1_w1", F32), (8, HC1, "l1_w2", F32),
    ]:
        w[name] = din(name, [nm_, 128, nk_ * 128], dty)
    biases_in = din("biases", [128, 96])
    iden_in = din("iden", [128, 128])
    headsel_in = din("headsel", [128, 2])
    maska_in = din("maska", [128, 128])
    maskb_in = din("maskb", [128, 128])
    cos_in = din("cos_fm", [128, T])
    sin_in = din("sin_fm", [128, T])

    out_ap = nc.dram_tensor("out_fm", [D, T], F32, kind="ExternalOutput").ap()
    taps = [nc.dram_tensor(f"tap{i}", [D, T], F32, kind="ExternalOutput").ap()
            for i in range(n_taps)]

    if dbg_attn >= 0:
        dbg_q = nc.dram_tensor("dbg_q", [G, 128, T], F32, kind="ExternalOutput").ap()
        dbg_kv = nc.dram_tensor("dbg_kv", [2 * (G * 128 * T + G * 16 * 128 * 130)], F32,
                                kind="ExternalOutput").ap()
        dbg_at = nc.dram_tensor("dbg_at", [G, 128, T], F32, kind="ExternalOutput").ap()
        dbg_neg = nc.dram_tensor("dbg_neg", [128, 16], F32, kind="ExternalOutput").ap()
        dbg_rows = nc.dram_tensor("dbg_rows", [32, 512], F32, kind="ExternalOutput").ap()
        dbg_e = nc.dram_tensor("dbg_e", [2, 128, 512], F32, kind="ExternalOutput").ap()
    q_hbm = nc.dram_tensor("q_hbm", [G, 128, T], F32).ap()
    attn_hbm = nc.dram_tensor("attn_hbm", [G, 128, T], F32).ap()
    KV_K = G * 128 * T
    KV_V = G * 16 * 128 * 130
    KVTOT = KV_K + KV_V
    kv_in = nc.dram_tensor("kv_in", [KVTOT], F32).ap()
    kv_all = nc.dram_tensor("kv_all", [2 * KVTOT], F32).ap()

    def kvK_in(fc):
        return kv_in[fc * 128 * T:(fc + 1) * 128 * T].rearrange("(p c) -> p c", p=128)

    def kvV_in(g, t16):
        o = KV_K + (g * 16 + t16) * 128 * 130
        return kv_in[o:o + 128 * 130].rearrange("(p c) -> p c", p=128)

    def kvK_all(r, fc):
        o = r * KVTOT + fc * 128 * T
        return kv_all[o:o + 128 * T].rearrange("(p c) -> p c", p=128)

    def kvV_all(r, g, t16):
        o = r * KVTOT + KV_K + (g * 16 + t16) * 128 * 130
        return kv_all[o:o + 128 * 130].rearrange("(p c) -> p c", p=128)

    es = contextlib.ExitStack()

    def sb(name, cols, dty=F32, p=128):
        return es.enter_context(nc.sbuf_tensor(name, [p, cols], dty))

    x_sb = sb("x_sb", DC * T)
    ar = sb("ar", 28672)
    wA = sb("wA", 2048)
    wB = sb("wB", 2048)
    wD = sb("wD", 2048)
    iden = sb("iden_sb", 128)
    headsel = sb("headsel_sb", 2)
    diagm = sb("diagm", 128)
    maskx = sb("maskx_sb", 128)
    ones128 = sb("ones128", 1)
    ones1 = es.enter_context(nc.sbuf_tensor("ones1", [1, 128], F32))
    bias_sb = sb("bias_sb", 96)
    b2h_sb = sb("b2h_sb", 8)
    dbgneg_sb = sb("dbgneg_sb", 16)
    r0 = es.enter_context(nc.sbuf_tensor("r0", [1, 512], F32))
    r1_ = es.enter_context(nc.sbuf_tensor("r1", [1, 512], F32))
    mh_sb = es.enter_context(nc.sbuf_tensor("mh_sb", [1, 64], F32))
    negm_sb = sb("negm_sb", 2)

    ps = [es.enter_context(nc.psum_tensor(f"ps{i}", [128, 512], F32)) for i in range(8)]

    sems = {k: es.enter_context(nc.semaphore(f"sem_{k}"))
            for k in ("t", "v", "s", "g", "XC", "WA0", "WA1", "WB0", "WB1",
                      "WD0", "WD1", "GA0", "GA1", "GB0", "GB1", "GC0", "GC1",
                      "Q0", "Q1", "K0", "K1", "V0", "V1",
                      "AT", "ST", "CC")}

    P = Prog()

    Bx = [[Buf() for _ in range(NTT)] for _ in range(DC)]
    Bps = [Buf() for _ in range(8)]
    Brings = {"wA": [Buf(), Buf()], "wB": [Buf(), Buf()], "wD": [Buf(), Buf()]}
    Bsmall = {k: Buf() for k in ("iden", "headsel", "diagm", "maskx", "ones128", "ones1",
                                 "bias", "b2h", "r0", "r1", "mh", "negm")}
    Bkv_in = Buf()
    Bkv_all = Buf()
    Bdbgneg = Buf()
    Bq_hbm = [Buf() for _ in range(G)]
    Battn_hbm = [Buf() for _ in range(G)]

    class Arena:
        def __init__(self):
            self.live = []

        def region(self, c0, c1):
            seed, keep = [], []
            for (a0, a1, b) in self.live:
                if a0 < c1 and c0 < a1:
                    seed += b.w + b.r
                    if not (c0 <= a0 and a1 <= c1):
                        keep.append((a0, a1, b))
                else:
                    keep.append((a0, a1, b))
            b = Buf(seed)
            self.live = keep + [(c0, c1, b)]
            return b

    AR = Arena()

    def E(eng, fn, reads=(), writes=(), waits=(), inc=1, stream=None):
        return P.emit(eng, fn, reads, writes, waits, inc, stream)

    def dma_load(dst_ap, src_ap, stream, reads=(), writes=()):
        return E("sync", lambda e, d=dst_ap, s=src_ap: e.dma_start(out=d, in_=s),
                 reads=reads, writes=writes, inc=16, stream=stream)

    def dma_g(dst_ap, src_ap, stream, reads=(), writes=()):
        return E("gpsimd", lambda e, d=dst_ap, s=src_ap: e.dma_start(out=d, in_=s),
                 reads=reads, writes=writes, inc=16, stream=stream)

    def mm(out_ap_, lhs_ap, rhs_ap, bank, first, last, reads):
        return E("tensor", lambda e, o=out_ap_, l=lhs_ap, r=rhs_ap, f=first, la=last:
                 e.matmul(o, l, r, start=f, stop=la),
                 reads=list(reads), writes=[Bps[bank]])

    E("gpsimd", lambda e: e.memset(ones128[:], 1.0), writes=[Bsmall["ones128"]], stream="g")
    E("gpsimd", lambda e: e.memset(ones1[:], 1.0), writes=[Bsmall["ones1"]], stream="g")
    dma_load(iden[:], iden_in[:, :], "XC", writes=[Bsmall["iden"]])
    dma_load(headsel[:], headsel_in[:, :], "XC", writes=[Bsmall["headsel"]])
    dma_load(diagm[:], maska_in[:, :], "XC", writes=[Bsmall["diagm"]])
    dma_load(maskx[:], maskb_in[:, :], "XC", writes=[Bsmall["maskx"]])
    dma_load(bias_sb[:], biases_in[:, :], "XC", writes=[Bsmall["bias"]])
    E("vector", lambda e: e.tensor_scalar_mul(b2h_sb[:], bias_sb[:, 72:80], 0.5),
      reads=[Bsmall["bias"]], writes=[Bsmall["b2h"]])
    for dc in range(DC):
        for tt in range(NTT):
            dma_load(x_sb[:, dc * T + tt * 512: dc * T + (tt + 1) * 512],
                     x_in[dc * 128:(dc + 1) * 128, tt * 512:(tt + 1) * 512],
                     "XC", writes=[Bx[dc][tt]])

    def xs(dc, tt):
        return x_sb[:, dc * T + tt * 512: dc * T + (tt + 1) * 512]

    SWAP_MASK = []
    for i in range(0, 32, 2):
        SWAP_MASK += [i + 1, i]

    def ln_stats(c_m, c_r, c_sq, scale_rstd=1.0):
        Bm = AR.region(c_m, c_m + T)
        Br = AR.region(c_r, c_r + T)
        Bsq = AR.region(c_sq, c_sq + 512)

        def mrep(tt):
            return ar[:, c_m + tt * 512: c_m + (tt + 1) * 512]

        def rrep(tt):
            return ar[:, c_r + tt * 512: c_r + (tt + 1) * 512]

        sqv = ar[:, c_sq:c_sq + 512]
        for tt in range(NTT):
            for dc in range(DC):
                E("scalar", lambda e, dc=dc, tt=tt: e.activation(sqv, xs(dc, tt), AF.Square),
                  reads=[Bx[dc][tt]], writes=[Bsq])
                mm(ps[0][0:1, :], ones128[:], xs(dc, tt), 0, dc == 0, dc == DC - 1,
                   [Bsmall["ones128"], Bx[dc][tt]])
                mm(ps[1][0:1, :], ones128[:], sqv, 1, dc == 0, dc == DC - 1,
                   [Bsmall["ones128"], Bsq])
            E("vector", lambda e: e.tensor_scalar_mul(r0[:], ps[0][0:1, :], 1.0 / D),
              reads=[Bps[0]], writes=[Bsmall["r0"]])
            E("vector", lambda e: e.tensor_scalar_mul(r1_[:], ps[1][0:1, :], 1.0 / D),
              reads=[Bps[1]], writes=[Bsmall["r1"]])
            E("vector", lambda e: e.tensor_mul(ps[0][0:1, :], r0[:], r0[:]),
              reads=[Bsmall["r0"]], writes=[Bps[0]])
            E("vector", lambda e: e.tensor_sub(r1_[:], r1_[:], ps[0][0:1, :]),
              reads=[Bps[0]], writes=[Bsmall["r1"]])
            E("vector", lambda e: e.tensor_scalar_add(r1_[:], r1_[:], EPS),
              writes=[Bsmall["r1"]])
            E("vector", lambda e: e.reciprocal(r1_[:], r1_[:]), writes=[Bsmall["r1"]])
            E("scalar", lambda e: e.activation(r1_[:], r1_[:], AF.Sqrt),
              writes=[Bsmall["r1"]])
            if scale_rstd != 1.0:
                E("vector", lambda e, sc=scale_rstd: e.tensor_scalar_mul(r1_[:], r1_[:], sc),
                  writes=[Bsmall["r1"]])
            E("tensor", lambda e: e.matmul(ps[2][:, :], ones1[:, :], r0[:], start=True, stop=True),
              reads=[Bsmall["ones1"], Bsmall["r0"]], writes=[Bps[2]])
            E("tensor", lambda e: e.matmul(ps[3][:, :], ones1[:, :], r1_[:], start=True, stop=True),
              reads=[Bsmall["ones1"], Bsmall["r1"]], writes=[Bps[3]])
            E("vector", lambda e, tt=tt: e.tensor_copy(mrep(tt), ps[2][:, :]),
              reads=[Bps[2]], writes=[Bm])
            E("vector", lambda e, tt=tt: e.tensor_copy(rrep(tt), ps[3][:, :]),
              reads=[Bps[3]], writes=[Br])
        return Bm, Br, mrep, rrep

    def xhat_tt(c_x, tt, Bm, Br, mrep, rrep):
        Bxh = AR.region(c_x, c_x + DC * 512)

        def xh(dc):
            return ar[:, c_x + dc * 512: c_x + (dc + 1) * 512]

        for dc in range(DC):
            E("vector", lambda e, dc=dc, tt=tt: e.tensor_sub(xh(dc), xs(dc, tt), mrep(tt)),
              reads=[Bx[dc][tt], Bm], writes=[Bxh])
            E("vector", lambda e, dc=dc: e.tensor_mul(xh(dc), xh(dc), rrep(tt)),
              reads=[Br], writes=[Bxh])
        return xh, Bxh

    def ffn(L):
        if L == 0:
            HC, wg, wu, wd = HC0, w["l0_wg"], w["l0_wu"], w["l0_wd"]
        else:
            HC, wg, wu, wd = HC1, w["l1_w1"], None, w["l1_w2"]
        c_h, c_xh, c_sq, c_m, c_r, c_t1 = 0, 16384, 20480, 21504, 23552, 25600
        Bh = AR.region(c_h, c_h + HC * 512)
        Bt1 = AR.region(c_t1, c_t1 + 512)

        def hv(hc):
            return ar[:, c_h + hc * 512: c_h + (hc + 1) * 512]

        t1 = ar[:, c_t1:c_t1 + 512]
        Bm, Br, mrep, rrep = ln_stats(c_m, c_r, c_sq, scale_rstd=(C127 if L == 0 else 1.0))
        ld = dma_g if L == 0 else dma_load
        for tb in range(NTT):
            xh, Bxh = xhat_tt(c_xh, tb, Bm, Br, mrep, rrep)
            for hc in range(HC):
                slot = hc % 2
                wav = wA[:, slot * 1024:(slot + 1) * 1024]
                Bwa = Brings["wA"][slot]
                ld(wav, wg[hc], (f"GA{slot}" if L == 0 else f"WA{slot}"), writes=[Bwa])
                gb = 4 + (hc % 2)
                for dc in range(DC):
                    mm(ps[gb][:, :], wav[:, dc * 128:(dc + 1) * 128], xh(dc), gb,
                       dc == 0, dc == DC - 1, [Bwa, Bxh])
                if L == 0:
                    wbv = wB[:, slot * 1024:(slot + 1) * 1024]
                    Bwb = Brings["wB"][slot]
                    ld(wbv, wu[hc], f"GB{slot}", writes=[Bwb])
                    ub = 6 + (hc % 2)
                    for dc in range(DC):
                        mm(ps[ub][:, :], wbv[:, dc * 128:(dc + 1) * 128], xh(dc), ub,
                           dc == 0, dc == DC - 1, [Bwb, Bxh])
                    E("scalar", lambda e, gb=gb: e.activation(t1, ps[gb][:, :], AF.Silu),
                      reads=[Bps[gb]], writes=[Bt1])
                    E("vector", lambda e, hc=hc, ub=ub: e.tensor_mul(hv(hc), t1, ps[ub][:, :]),
                      reads=[Bt1, Bps[ub]], writes=[Bh])
                else:
                    E("scalar", lambda e, hc=hc, gb=gb: e.activation(
                        hv(hc), ps[gb][:, :], AF.Gelu, bias=bias_sb[:, 40 + hc:41 + hc]),
                      reads=[Bps[gb], Bsmall["bias"]], writes=[Bh])
            for oc in range(DC):
                db = 2 + (oc % 2)
                for sg in range(HC // 8):
                    slot = sg % 2
                    wdv = wD[:, slot * 1024:(slot + 1) * 1024]
                    Bwd = Brings["wD"][slot]
                    ld(wdv, wd[oc][:, sg * 1024:(sg + 1) * 1024], (f"GC{slot}" if L == 0 else f"WD{slot}"), writes=[Bwd])
                    for j in range(8):
                        hc = sg * 8 + j
                        mm(ps[db][:, :], wdv[:, j * 128:(j + 1) * 128], hv(hc), db,
                           hc == 0, hc == HC - 1, [Bwd, Bh])
                if L == 0:
                    E("vector", lambda e, oc=oc, tb=tb, db=db: e.scalar_tensor_tensor(
                        xs(oc, tb), ps[db][:, :], 0.5 * C127, xs(oc, tb), ALU.mult, ALU.add),
                      reads=[Bps[db]], writes=[Bx[oc][tb]])
                else:
                    E("scalar", lambda e, oc=oc, db=db: e.activation(
                        t1, ps[db][:, :], AF.Identity, bias=b2h_sb[:, oc:oc + 1], scale=0.5),
                      reads=[Bps[db], Bsmall["b2h"]], writes=[Bt1])
                    E("vector", lambda e, oc=oc, tb=tb: e.tensor_add(xs(oc, tb), xs(oc, tb), t1),
                      reads=[Bt1], writes=[Bx[oc][tb]])

    def attn(L):
        if L == 0:
            wq, wk, wv, wo = w["l0_wq"], w["l0_wk"], w["l0_wv"], w["l0_wo"]
            bq_c, bk_c, bv_c, bo_c = 0, 8, 16, 24
            SC = SC0
        else:
            wq, wk, wv, wo = w["l1_wq"], w["l1_wk"], w["l1_wv"], w["l1_wo"]
            bq_c = bk_c = bv_c = None
            bo_c = 32
            SC = SC1

        if attn_parts < 1:
            return
        c_xh, c_sq, c_m, c_r = 0, 4096, 4608, 6656
        c_cos, c_sin = 8704, 10752
        c_ks, c_vs, c_vt, c_t2 = 12800, 13312, 13824, 14596
        Bm, Br, mrep, rrep = ln_stats(c_m, c_r, c_sq)
        if attn_parts < 2:
            return
        Bcos = AR.region(c_cos, c_cos + T)
        Bsin = AR.region(c_sin, c_sin + T)

        def cosv(tt):
            return ar[:, c_cos + tt * 512: c_cos + (tt + 1) * 512]

        def sinv(tt):
            return ar[:, c_sin + tt * 512: c_sin + (tt + 1) * 512]

        if L == 1:
            dma_load(ar[:, c_cos:c_cos + T], cos_in[:, :], "XC", writes=[Bcos])
            dma_load(ar[:, c_sin:c_sin + T], sin_in[:, :], "XC", writes=[Bsin])
        Bks = AR.region(c_ks, c_ks + 512)
        Bvs = AR.region(c_vs, c_vs + 512)
        Bvt = AR.region(c_vt, c_vt + 260)
        Bt2 = AR.region(c_t2, c_t2 + 512)
        ksv = ar[:, c_ks:c_ks + 512]
        vsv = ar[:, c_vs:c_vs + 512]
        t2 = ar[:, c_t2:c_t2 + 512]
        for sl in (0, 1):
            for cc in (64, 129):
                col = c_vt + sl * 130 + cc
                E("gpsimd", lambda e, col=col: e.memset(ar[:, col:col + 1], 1.0),
                  writes=[Bvt], stream="g")

        def rope(dst, Bdst, tt):
            E("vector", lambda e, dst=dst: e.stream_shuffle(t2, dst, SWAP_MASK),
              reads=[Bdst], writes=[Bt2])
            E("vector", lambda e, tt=tt: e.tensor_mul(t2, t2, sinv(tt)),
              reads=[Bsin], writes=[Bt2])
            E("vector", lambda e, dst=dst, tt=tt: e.tensor_mul(dst, dst, cosv(tt)),
              reads=[Bcos], writes=[Bdst])
            E("vector", lambda e, dst=dst: e.tensor_add(dst, dst, t2),
              reads=[Bt2], writes=[Bdst])

        for tt in range(NTT):
            xh, Bxh = xhat_tt(c_xh, tt, Bm, Br, mrep, rrep)
            if attn_parts < 3:
                continue
            for proj in ("k", "v", "q"):
                wmat = {"k": wk, "v": wv, "q": wq}[proj]
                b_c = {"k": bk_c, "v": bv_c, "q": bq_c}[proj]
                stage, Bstage = (vsv, Bvs) if proj == "v" else (ksv, Bks)
                for oc in range(DC):
                    slot = oc % 2
                    wav = wB[:, slot * 1024:(slot + 1) * 1024]
                    Bwa = Brings["wB"][slot]
                    dma_load(wav, wmat[oc], f"WB{slot}", writes=[Bwa])
                    pb = 4 + (oc % 2)
                    for dc in range(DC):
                        mm(ps[pb][:, :], wav[:, dc * 128:(dc + 1) * 128], xh(dc), pb,
                           dc == 0, dc == DC - 1, [Bwa, Bxh])
                    if b_c is not None:
                        E("scalar", lambda e, pb=pb, b_c=b_c, oc=oc, stage=stage: e.activation(
                            stage, ps[pb][:, :], AF.Identity,
                            bias=bias_sb[:, b_c + oc:b_c + oc + 1]),
                          reads=[Bps[pb], Bsmall["bias"]], writes=[Bstage])
                    else:
                        E("scalar", lambda e, pb=pb, stage=stage: e.activation(
                            stage, ps[pb][:, :], AF.Identity),
                          reads=[Bps[pb]], writes=[Bstage])
                    if L == 1 and proj in ("k", "q"):
                        rope(stage, Bstage, tt)
                    if proj == "k":
                        dma_g(kvK_in(oc)[:, tt * 512:(tt + 1) * 512], ksv[:, :], "ST",
                              reads=[Bks], writes=[Bkv_in])
                    elif proj == "q":
                        dma_g(q_hbm[oc, :, tt * 512:(tt + 1) * 512], ksv[:, :], "ST",
                              reads=[Bks], writes=[Bq_hbm[oc]])
                    else:
                        for i in range(4):
                            tb_ = 2 + (i % 2)
                            E("tensor", lambda e, i=i, tb_=tb_: e.transpose(
                                ps[tb_][:, :128], vsv[:, i * 128:(i + 1) * 128], iden[:]),
                              reads=[Bvs, Bsmall["iden"]], writes=[Bps[tb_]])
                            sl = i % 2
                            vt = ar[:, c_vt + sl * 130: c_vt + (sl + 1) * 130]
                            E("vector", lambda e, tb_=tb_, vt=vt: e.tensor_copy(
                                vt[:, 0:64], ps[tb_][:, 0:64]),
                              reads=[Bps[tb_]], writes=[Bvt])
                            E("vector", lambda e, tb_=tb_, vt=vt: e.tensor_copy(
                                vt[:, 65:129], ps[tb_][:, 64:128]),
                              reads=[Bps[tb_]], writes=[Bvt])
                            dma_g(kvV_in(oc, tt * 4 + i), vt[:, :], "ST",
                                  reads=[Bvt], writes=[Bkv_in])

        if attn_parts < 4:
            return
        E("gpsimd", lambda e: e.collective_compute(
            "AllGather", ALU.bypass,
            replica_groups=[[0, 1], [2, 3], [4, 5], [6, 7]],
            ins=[kv_in[:]], outs=[kv_all[:]]),
          reads=[Bkv_in], writes=[Bkv_all], stream="CC")

        sets = []
        for st_ in range(2):
            b = st_ * 10304
            sets.append({"q": b, "k": b + 2048, "v": b + 6144})
        c_e, c_dr, c_e0, c_st, c_cr = 20608, 23680, 25216, 26240, 27264
        Bq = [None, None]
        Bk = [None, None]
        Bv = [None, None]
        Be = [AR.region(c_e + i * 512, c_e + (i + 1) * 512) for i in range(4)]
        Bdr = AR.region(c_dr, c_dr + 1536)
        Be0 = AR.region(c_e0, c_e0 + 1024)
        Bst = AR.region(c_st, c_st + 1024)
        Bcr = AR.region(c_cr, c_cr + 1024)

        def ev(i):
            return ar[:, c_e + i * 512: c_e + (i + 1) * 512]

        def drow(i):
            return ar[0:1, c_dr + i * 512: c_dr + (i + 1) * 512]

        def e0row(h):
            return ar[0:1, c_e0 + h * 512: c_e0 + (h + 1) * 512]

        def stag(h):
            return ar[:, c_st + h * 512: c_st + (h + 1) * 512]

        def crep(i):
            return ar[:, c_cr + i * 512: c_cr + (i + 1) * 512]

        if L == 0:
            def cells_of(qb):
                return [(kt, None) for kt in range(KT)]
        else:
            def cells_of(qb):
                cl = []
                for kt in range(4 * qb + 4):
                    cl.append((kt, None if kt < 4 * qb else ("own", kt - 4 * qb)))
                for kt in range(4 * qb + 4):
                    cl.append((16 + kt, None if kt < 4 * qb else ("cross", kt - 4 * qb)))
                return cl

        for g in range(G):
            st_ = g % 2
            cq, ck, cv = sets[st_]["q"], sets[st_]["k"], sets[st_]["v"]
            Bq[st_] = AR.region(cq, cq + 2048)
            Bk[st_] = AR.region(ck, ck + 4096)
            Bv[st_] = AR.region(cv, cv + 4160)
            qg = ar[:, cq:cq + 2048]
            kg = ar[:, ck:ck + 4096]

            def vtile(kt, h, cv=cv):
                o = cv + kt * 130 + h * 65
                return ar[:, o:o + 65]

            dma_load(qg, q_hbm[g, :, :], f"Q{st_}", reads=[Bq_hbm[g]], writes=[Bq[st_]])
            for r in range(2):
                dma_load(kg[:, r * 2048:(r + 1) * 2048], kvK_all(r, g), f"K{st_}",
                         reads=[Bkv_all], writes=[Bk[st_]])
                for t16 in range(16):
                    dma_load(ar[:, cv + (r * 16 + t16) * 130: cv + (r * 16 + t16 + 1) * 130],
                             kvV_all(r, g, t16), f"V{st_}", reads=[Bkv_all], writes=[Bv[st_]])

            if attn_parts < 5:
                continue
            sqt = ev(0)
            for src, n_t, b0, Bsrc in ((qg, 4, 0, Bq[st_]), (kg, 8, 2, Bk[st_])):
                for tt in range(n_t):
                    E("scalar", lambda e, src=src, tt=tt: e.activation(
                        sqt, src[:, tt * 512:(tt + 1) * 512], AF.Square),
                      reads=[Bsrc], writes=[Be[0]])
                    for h in range(2):
                        mm(ps[h][0:1, :], headsel[:, h:h + 1], sqt, h, True, True,
                           [Bsmall["headsel"], Be[0]])
                        if tt == 0:
                            E("vector", lambda e, h=h, b0=b0: e.reduce_max(
                                mh_sb[0:1, b0 + h:b0 + h + 1], ps[h][0:1, :], axis=AX.X),
                              reads=[Bps[h]], writes=[Bsmall["mh"]])
                        else:
                            E("vector", lambda e, h=h: e.reduce_max(
                                mh_sb[0:1, 8 + h:9 + h], ps[h][0:1, :], axis=AX.X),
                              reads=[Bps[h]], writes=[Bsmall["mh"]])
                            E("vector", lambda e, h=h, b0=b0: e.tensor_max(
                                mh_sb[0:1, b0 + h:b0 + h + 1],
                                mh_sb[0:1, b0 + h:b0 + h + 1], mh_sb[0:1, 8 + h:9 + h]),
                              writes=[Bsmall["mh"]])
            for h in range(2):
                E("vector", lambda e, h=h: e.tensor_mul(
                    mh_sb[0:1, 4 + h:5 + h], mh_sb[0:1, h:h + 1], mh_sb[0:1, 2 + h:3 + h]),
                  writes=[Bsmall["mh"]])
                E("scalar", lambda e, h=h: e.activation(
                    mh_sb[0:1, 4 + h:5 + h], mh_sb[0:1, 4 + h:5 + h], AF.Sqrt),
                  writes=[Bsmall["mh"]])
                E("vector", lambda e, h=h: e.tensor_scalar_mul(
                    mh_sb[0:1, 4 + h:5 + h], mh_sb[0:1, 4 + h:5 + h], -SC),
                  writes=[Bsmall["mh"]])
                E("tensor", lambda e, h=h: e.matmul(
                    ps[h][:, 0:1], ones1[:, :], mh_sb[0:1, 4 + h:5 + h],
                    start=True, stop=True),
                  reads=[Bsmall["ones1"], Bsmall["mh"]], writes=[Bps[h]])
                E("vector", lambda e, h=h: e.tensor_copy(negm_sb[:, h:h + 1], ps[h][:, 0:1]),
                  reads=[Bps[h]], writes=[Bsmall["negm"]])
                if dbg_attn == L:
                    E("vector", lambda e, h=h, g=g: e.tensor_copy(
                        dbgneg_sb[:, 2 * g + h:2 * g + h + 1], negm_sb[:, h:h + 1]),
                      reads=[Bsmall["negm"]], writes=[Bdbgneg])

            if attn_parts < 6:
                continue
            for qb in range(QBN):
                cells = cells_of(qb)
                n_cells = len(cells)
                for ci, (kt, kind) in enumerate(cells):
                    sb_ = [0 + (ci % 2), 2 + (ci % 2)]
                    eslot = [(ci % 2) * 2, (ci % 2) * 2 + 1]
                    for h in range(2):
                        bb = sb_[h]
                        E("tensor", lambda e, h=h, kt=kt, qb=qb, bb=bb, kg=kg, qg=qg: e.matmul(
                            ps[bb][:, :],
                            kg[h * 64:(h + 1) * 64, kt * 128:(kt + 1) * 128],
                            qg[h * 64:(h + 1) * 64, qb * 512:(qb + 1) * 512],
                            start=True, stop=True, tile_position=(h * 64, 0)),
                          reads=[Bk[st_], Bq[st_]], writes=[Bps[bb]])
                    if kind is not None:
                        tag, sp = kind
                        for h in range(2):
                            bb = sb_[h]
                            for s_ in range(sp):
                                E("vector", lambda e, bb=bb, s_=s_: e.tensor_scalar_add(
                                    ps[bb][:, s_ * 128:(s_ + 1) * 128],
                                    ps[bb][:, s_ * 128:(s_ + 1) * 128], NEGBIG),
                                  writes=[Bps[bb]])
                            mt = diagm if tag == "own" else maskx
                            mb = Bsmall["diagm" if tag == "own" else "maskx"]
                            E("vector", lambda e, bb=bb, sp=sp, mt=mt: e.tensor_add(
                                ps[bb][:, sp * 128:(sp + 1) * 128],
                                ps[bb][:, sp * 128:(sp + 1) * 128], mt[:]),
                              reads=[mb], writes=[Bps[bb]])
                    for h in range(2):
                        bb, esl = sb_[h], eslot[h]
                        if dbg_attn == L and g == 0 and qb == 0 and ci == 0 and h == 0:
                            E("vector", lambda e, bb=bb: e.tensor_copy(stag(0), ps[bb][:, :]),
                              reads=[Bps[bb]], writes=[Bst])
                            dma_g(dbg_e[0, :, :], stag(0), "ST", reads=[Bst])
                        E("scalar", lambda e, bb=bb, esl=esl, h=h: e.activation(
                            ev(esl), ps[bb][:, :], AF.Exp,
                            bias=negm_sb[:, h:h + 1], scale=SC),
                          reads=[Bps[bb], Bsmall["negm"]], writes=[Be[esl]])
                        if dbg_attn == L and g == 0 and qb == 0 and ci == 0 and h == 0:
                            dma_g(dbg_e[1, :, :], ev(esl), "ST", reads=[Be[esl]])
                        if L == 0 and kt == 0:
                            E("vector", lambda e, esl=esl, h=h: e.tensor_copy(
                                e0row(h), ev(esl)[0:1, :]),
                              reads=[Be[esl]], writes=[Be0])
                        pvb = 4 + h
                        mm(ps[pvb][0:65, :], vtile(kt, h), ev(esl), pvb,
                           ci == 0, ci == n_cells - 1, [Bv[st_], Be[esl]])
                if attn_parts < 7:
                    continue
                for h in range(2):
                    pvb = 4 + h
                    if dbg_attn == L and g == 0:
                        sl = (qb * 2 + h) * 4
                        E("vector", lambda e, pvb=pvb: e.tensor_copy(r0[:], ps[pvb][64:65, :]),
                          reads=[Bps[pvb]], writes=[Bsmall["r0"]])
                        dma_g(dbg_rows[sl:sl + 1, :], r0[:], "ST", reads=[Bsmall["r0"]])
                        if L == 0:
                            dma_g(dbg_rows[sl + 1:sl + 2, :], e0row(h), "ST", reads=[Be0])
                        E("vector", lambda e, pvb=pvb: e.tensor_copy(r0[:], ps[pvb][0:1, :]),
                          reads=[Bps[pvb]], writes=[Bsmall["r0"]])
                        dma_g(dbg_rows[sl + 2:sl + 3, :], r0[:], "ST", reads=[Bsmall["r0"]])
                    if L == 0:
                        r1b = 6 + h
                        E("tensor", lambda e, r1b=r1b, v0=vtile(0, h)[0:1, :], er=e0row(h):
                          e.matmul(ps[r1b][0:65, :], v0, er, start=True, stop=True),
                          reads=[Bv[st_], Be0], writes=[Bps[r1b]])
                        E("vector", lambda e, pvb=pvb, h=h: e.tensor_sub(
                            drow(0), ps[pvb][64:65, :], e0row(h)),
                          reads=[Bps[pvb], Be0], writes=[Bdr])
                        E("vector", lambda e, pvb=pvb: e.tensor_scalar_mul(
                            drow(1), ps[pvb][64:65, :], 1e-9),
                          reads=[Bps[pvb]], writes=[Bdr])
                        E("vector", lambda e: e.tensor_max(drow(0), drow(0), drow(1)),
                          writes=[Bdr])
                        E("vector", lambda e, pvb=pvb: e.reciprocal(drow(1), ps[pvb][64:65, :]),
                          reads=[Bps[pvb]], writes=[Bdr])
                        E("vector", lambda e: e.reciprocal(drow(0), drow(0)), writes=[Bdr])
                        E("vector", lambda e: e.tensor_add(drow(2), drow(1), drow(0)),
                          writes=[Bdr])
                        E("vector", lambda e: e.tensor_scalar_mul(drow(2), drow(2), 0.5),
                          writes=[Bdr])
                        E("vector", lambda e: e.tensor_scalar_mul(drow(0), drow(0), 0.5),
                          writes=[Bdr])
                        E("tensor", lambda e: e.matmul(
                            ps[0][:, :], ones1[:, :], drow(2), start=True, stop=True),
                          reads=[Bsmall["ones1"], Bdr], writes=[Bps[0]])
                        E("tensor", lambda e: e.matmul(
                            ps[1][:, :], ones1[:, :], drow(0), start=True, stop=True),
                          reads=[Bsmall["ones1"], Bdr], writes=[Bps[1]])
                        E("vector", lambda e: e.tensor_copy(crep(0), ps[0][:, :]),
                          reads=[Bps[0]], writes=[Bcr])
                        E("vector", lambda e: e.tensor_copy(crep(1), ps[1][:, :]),
                          reads=[Bps[1]], writes=[Bcr])
                        E("vector", lambda e, pvb=pvb, h=h: e.tensor_mul(
                            stag(h)[0:64, :], ps[pvb][0:64, :], crep(0)[0:64, :]),
                          reads=[Bps[pvb], Bcr], writes=[Bst])
                        E("vector", lambda e, r1b=r1b: e.tensor_mul(
                            crep(0)[0:64, :], ps[r1b][0:64, :], crep(1)[0:64, :]),
                          reads=[Bps[r1b]], writes=[Bcr])
                        E("vector", lambda e, h=h: e.tensor_sub(
                            stag(h)[0:64, :], stag(h)[0:64, :], crep(0)[0:64, :]),
                          reads=[Bcr], writes=[Bst])
                    else:
                        E("vector", lambda e, pvb=pvb: e.reciprocal(drow(0), ps[pvb][64:65, :]),
                          reads=[Bps[pvb]], writes=[Bdr])
                        E("tensor", lambda e: e.matmul(
                            ps[0][:, :], ones1[:, :], drow(0), start=True, stop=True),
                          reads=[Bsmall["ones1"], Bdr], writes=[Bps[0]])
                        E("vector", lambda e: e.tensor_copy(crep(0), ps[0][:, :]),
                          reads=[Bps[0]], writes=[Bcr])
                        E("vector", lambda e, pvb=pvb, h=h: e.tensor_mul(
                            stag(h)[0:64, :], ps[pvb][0:64, :], crep(0)[0:64, :]),
                          reads=[Bps[pvb], Bcr], writes=[Bst])
                    dma_g(attn_hbm[g, h * 64:(h + 1) * 64, qb * 512:(qb + 1) * 512],
                          stag(h)[0:64, :], "ST", reads=[Bst], writes=[Battn_hbm[g]])

        if dbg_attn == L:
            dma_g(dbg_neg[:, :], dbgneg_sb[:], "ST", reads=[Bdbgneg])
            for gg in range(G):
                dma_g(dbg_q[gg, :, :], q_hbm[gg, :, :], "ST", reads=[Bq_hbm[gg]])
                dma_g(dbg_at[gg, :, :], attn_hbm[gg, :, :], "ST", reads=[Battn_hbm[gg]])
            dma_g(dbg_kv[:], kv_all[:], "ST", reads=[Bkv_all])

        if attn_parts < 8:
            return
        c_at = 0
        for qb in range(QBN):
            Bat = AR.region(c_at, c_at + 4096)

            def at(fc):
                return ar[:, c_at + fc * 512: c_at + (fc + 1) * 512]

            for fc in range(G):
                dma_load(at(fc), attn_hbm[fc, :, qb * 512:(qb + 1) * 512], "AT",
                         reads=[Battn_hbm[fc]], writes=[Bat])
            for oc in range(DC):
                slot = oc % 2
                wav = wD[:, slot * 1024:(slot + 1) * 1024]
                Bwa = Brings["wD"][slot]
                dma_load(wav, wo[oc], f"WD{slot}", writes=[Bwa])
                pb = 4 + (oc % 2)
                for fc in range(G):
                    mm(ps[pb][:, :], wav[:, fc * 128:(fc + 1) * 128], at(fc), pb,
                       fc == 0, fc == G - 1, [Bwa, Bat])
                E("vector", lambda e, oc=oc, qb=qb, pb=pb: e.scalar_tensor_tensor(
                    xs(oc, qb), ps[pb][:, :], bias_sb[:, bo_c + oc:bo_c + oc + 1],
                    xs(oc, qb), ALU.add, ALU.add),
                  reads=[Bps[pb], Bsmall["bias"]], writes=[Bx[oc][qb]])

    tap_i = [0]

    def tap():
        if tap_i[0] < n_taps:
            tp = taps[tap_i[0]]
            for dc in range(DC):
                dma_g(tp[dc * 128:(dc + 1) * 128, :], x_sb[:, dc * T:(dc + 1) * T], "ST",
                      reads=[Bx[dc][0], Bx[dc][1], Bx[dc][2], Bx[dc][3]])
            tap_i[0] += 1

    sub_i = 0
    for L in (0, 1):
        for kind in ("ffn", "attn", "ffn"):
            if sub_i >= n_sub:
                break
            if kind == "ffn":
                ffn(L)
            else:
                attn(L)
            sub_i += 1
            tap()

    st_toks = []
    for dc in range(DC):
        st_toks.append(dma_g(out_ap[dc * 128:(dc + 1) * 128, :], x_sb[:, dc * T:(dc + 1) * T],
                             "ST", reads=[Bx[dc][0], Bx[dc][1], Bx[dc][2], Bx[dc][3]]))
    E("gpsimd", lambda e: e.memset(b2h_sb[:], 0.0),
      writes=[Bsmall["b2h"]], waits=st_toks, stream="g")

    P.assign_vals()
    engines = {}
    with nc.Block() as block:
        @block.sync
        def _(eng):
            P.emit_engine("sync", eng, sems)

        @block.tensor
        def _(eng):
            P.emit_engine("tensor", eng, sems)

        @block.vector
        def _(eng):
            P.emit_engine("vector", eng, sems)

        @block.scalar
        def _(eng):
            P.emit_engine("scalar", eng, sems)

        @block.gpsimd
        def _(eng):
            P.emit_engine("gpsimd", eng, sems)

    es.close()
    return nc


def _tile_img(WT, nk, nm):
    a = np.ascontiguousarray(
        WT.reshape(nk, 128, nm, 128).transpose(2, 1, 0, 3).reshape(nm, 128, nk * 128))
    return a


def _host_prep(inputs):
    f32 = np.float32
    x = np.asarray(inputs["L0_attn_x"], dtype=f32)
    shared = {}
    shared["l0_wq"] = _tile_img(np.asarray(inputs["L0_attn_Wq"], f32).T, 8, 8)
    shared["l0_wk"] = _tile_img(np.asarray(inputs["L0_attn_Wk"], f32).T, 8, 8)
    shared["l0_wv"] = _tile_img(np.asarray(inputs["L0_attn_Wv"], f32).T, 8, 8)
    shared["l0_wo"] = _tile_img(np.asarray(inputs["L0_attn_Wo"], f32).T, 8, 8)
    shared["l0_wg"] = _tile_img(np.asarray(inputs["L0_ffn_w_gate"], np.int32).T, 8, HC0)
    shared["l0_wu"] = _tile_img(np.asarray(inputs["L0_ffn_w_up"], np.int32).T, 8, HC0)
    shared["l0_wd"] = _tile_img(np.asarray(inputs["L0_ffn_w_down"], np.int32).T, HC0, 8)
    shared["l1_wq"] = _tile_img(np.asarray(inputs["L1_attn_Wq"], f32).T, 8, 8)
    shared["l1_wk"] = _tile_img(np.asarray(inputs["L1_attn_Wk"], f32).T, 8, 8)
    shared["l1_wv"] = _tile_img(np.asarray(inputs["L1_attn_Wv"], f32).T, 8, 8)
    shared["l1_wo"] = _tile_img(np.asarray(inputs["L1_attn_Wo"], f32).T, 8, 8)
    shared["l1_w1"] = _tile_img(np.asarray(inputs["L1_ffn_w1"], f32), 8, HC1)
    shared["l1_w2"] = _tile_img(np.asarray(inputs["L1_ffn_w2"], f32), HC1, 8)

    biases = np.zeros((128, 96), dtype=f32)
    biases[:, 0:8] = np.asarray(inputs["L0_attn_bq"], f32).reshape(8, 128).T
    biases[:, 8:16] = np.asarray(inputs["L0_attn_bk"], f32).reshape(8, 128).T
    biases[:, 16:24] = np.asarray(inputs["L0_attn_bv"], f32).reshape(8, 128).T
    biases[:, 24:32] = np.asarray(inputs["L0_attn_bo"], f32).reshape(8, 128).T
    biases[:, 32:40] = np.asarray(inputs["L1_attn_bo"], f32).reshape(8, 128).T
    biases[:, 40:72] = np.asarray(inputs["L1_ffn_b1"], f32).reshape(32, 128).T
    biases[:, 72:80] = np.asarray(inputs["L1_ffn_b2"], f32).reshape(8, 128).T
    shared["biases"] = biases
    shared["iden"] = np.eye(128, dtype=f32)
    hs = np.zeros((128, 2), dtype=f32)
    hs[0:64, 0] = 1.0
    hs[64:128, 1] = 1.0
    shared["headsel"] = hs
    kk, qq = np.meshgrid(np.arange(128), np.arange(128), indexing="ij")
    diagtile = np.where(kk > qq, np.float32(NEGBIG), np.float32(0.0)).astype(f32)

    inv_freq = 1.0 / (10000.0 ** (np.arange(0, 64, 2, dtype=np.float64) / 64.0))

    in_maps = []
    unit_rows = []
    for core in range(8):
        seq, par = core // 2, core % 2
        units = [2 * u + par for u in range(16)]
        rows = np.concatenate([np.arange(u * 128, (u + 1) * 128) for u in units])
        unit_rows.append((seq, rows))
        xc = np.ascontiguousarray(x[seq][rows].T)
        pos = rows.astype(np.float64)
        ang = pos[None, :] * inv_freq[:, None]
        cosr = np.cos(ang)
        sinr = np.sin(ang)
        cos_fm = np.empty((128, T), dtype=f32)
        sin_fm = np.empty((128, T), dtype=f32)
        for p in range(128):
            c = p % 64
            i = c // 2
            cos_fm[p] = cosr[i].astype(f32)
            sin_fm[p] = (-sinr[i] if c % 2 == 0 else sinr[i]).astype(f32)
        maska = diagtile if par == 0 else np.zeros((128, 128), f32)
        maskb = (np.full((128, 128), NEGBIG, f32) if par == 0 else diagtile)
        m = dict(shared)
        m["x_fm"] = xc
        m["cos_fm"] = cos_fm
        m["sin_fm"] = sin_fm
        m["maska"] = maska
        m["maskb"] = maskb
        in_maps.append(m)
    return in_maps, unit_rows


def kernel(**inputs):
    in_maps, unit_rows = _host_prep(inputs)
    nc = build_program(0)
    res = run_bass_kernel_spmd(nc, in_maps, core_ids=list(range(8)))
    y = np.empty((4, S, D), dtype=np.float32)
    for core in range(8):
        seq, rows = unit_rows[core]
        y[seq][rows] = np.ascontiguousarray(res.results[core]["out_fm"]).T
    return y
